# version 02
# baseline (speedup 1.0000x reference)
import contextlib
import math

import numpy as np

import concourse.bass as bass
import concourse.mybir as mybir
from concourse.bass_utils import run_bass_kernel_spmd

F32 = mybir.dt.float32
BF16 = mybir.dt.bfloat16
AF = mybir.ActivationFunctionType
ALU = mybir.AluOpType
AX = mybir.AxisListType.X

L, D, NH, FD, HD, NE, HID = 2048, 1024, 16, 16, 64, 8, 2048
NKD = D // 128
TCH = L // 128
NTQ = L // 512
NM = HID // 128
TFF = 1024
NTF = L // TFF
NSUB = TFF // 512
RS32 = 1.0 / math.sqrt(32.0)
RS2 = 1.0 / math.sqrt(2.0)

PHASE = 99
DMA_SEM = {"G": "DG", "S": "DS", "A": "DA"}
ENG_SEM = {"T": "T", "V": "V", "A": "A", "G": "G"}


class Sched:

    def __init__(self):
        self.recs = []
        self.cnt = {s: 0 for s in ["T", "V", "A", "G", "DG", "DS", "DA"]}

    def op(self, eng, fn, waits=(), inc=None):
        ws = [(s, v) for (s, v) in waits if v and v > 0]
        for s, v in ws:
            assert v <= self.cnt[s], (s, v, self.cnt[s])
        val = None
        if inc is not None:
            self.cnt[inc] += 16 if inc.startswith("D") else 1
            val = self.cnt[inc]
        self.recs.append((eng, fn, ws, inc))
        return val

    def dma(self, q, out_ap, in_ap, waits=()):
        sem = DMA_SEM[q]
        return self.op(q if q != "A" else "A",
                       lambda e, o=out_ap, i=in_ap: e.dma_start(out=o, in_=i),
                       waits, inc=sem)


def build():
    nc = bass.Bass("TRN2", target_bir_lowering=False)
    es = contextlib.ExitStack()

    d = {}
    d["x"] = nc.dram_tensor("x", [L, D], F32, kind="ExternalInput").ap()
    d["Wq"] = nc.dram_tensor("Wq", [256, D], F32, kind="ExternalInput").ap()
    d["Wk"] = nc.dram_tensor("Wk", [256, D], F32, kind="ExternalInput").ap()
    d["Wv"] = nc.dram_tensor("Wv", [D, D], F32, kind="ExternalInput").ap()
    d["Wo"] = nc.dram_tensor("Wo", [D, D], F32, kind="ExternalInput").ap()
    d["Wg"] = nc.dram_tensor("Wg", [D, NE], F32, kind="ExternalInput").ap()
    d["W12r"] = nc.dram_tensor("W12r", [NE, NM, 2, 128, NKD, 128], F32, kind="ExternalInput").ap()
    d["W3r"] = nc.dram_tensor("W3r", [NE, NM, 128, D], F32, kind="ExternalInput").ap()
    d["cIdent"] = nc.dram_tensor("cIdent", [128, 128], F32, kind="ExternalInput").ap()
    d["cHalf"] = nc.dram_tensor("cHalf", [128, 5, 512], F32, kind="ExternalInput").ap()
    d["cSel"] = nc.dram_tensor("cSel", [8, NE, 128], F32, kind="ExternalInput").ap()
    d["cOnes"] = nc.dram_tensor("cOnes", [128, 68], F32, kind="ExternalInput").ap()
    d["out"] = nc.dram_tensor("out", [L, D], F32, kind="ExternalOutput").ap()
    dbg = {}
    if PHASE < 99:
        dbg["dbg_st"] = nc.dram_tensor("dbg_st", [128, 48], F32, kind="ExternalOutput").ap()
        for nm, shape in [("dbg_xnfm", [128, NKD * L]), ("dbg_q", [128, 4 * L]),
                          ("dbg_k", [128, 4 * L]), ("dbg_v", [128, TCH * NH * 65]),
                          ("dbg_comb", [32, TCH * 128]), ("dbg_acc", [128, NKD * L]),
                          ("dbg_yfm", [128, NKD * L])]:
            dbg[nm] = nc.dram_tensor(nm, shape, F32, kind="ExternalOutput").ap()

    SZ = {F32: 4, BF16: 2}
    SB_BASE = (nc.sbuf_base + 31) & ~31
    cur = [SB_BASE]

    def sb(name, shape, dtype, at=None):
        nb = int(np.prod(shape[1:])) * SZ[dtype]
        if at is None:
            at = cur[0]
            cur[0] = (at + nb + 31) & ~31
        else:
            assert at + nb <= nc.SBUF_PARTITION_SIZE_BYTES, (name, at, nb)
        return nc.alloc_sbuf_tensor_at(name, list(shape), dtype, offset=at), at

    xnfm, _ = sb("s_xnfm", [128, NKD, L], BF16)
    identb, _ = sb("s_identb", [128, 128], BF16)
    identf, _ = sb("s_identf", [128, 128], F32)
    halfb, _ = sb("s_halfb", [128, 5, 512], BF16)
    self_, _ = sb("s_sel", [8, NE, 128], F32)
    onesb, _ = sb("s_onesb", [128, 68], BF16)
    onesf, _ = sb("s_onesf", [128, 68], F32)
    wg_sb, _ = sb("s_wg", [128, NKD, NE], BF16)
    comb_em, _ = sb("s_combem", [32, TCH, 128], F32)
    zbuf, _ = sb("s_zbuf", [128, 512], F32)
    zrep, _ = sb("s_zrep", [128, 512], F32)
    lg_tm, _ = sb("s_lgtm", [128, TCH, NE], F32)
    stat_t = {}
    for nm in ["s1", "mean_all", "s2_all", "var_all", "std_all", "istd_all", "tmp16",
               "gm1", "gm2", "ge2", "gden", "gw1", "gw2", "cell0", "cell1"]:
        stat_t[nm], _ = sb(f"s_st_{nm}", [128, 16], F32)
    eq1_all, _ = sb("s_eq1a", [128, 16, 8], F32)
    l2_all, _ = sb("s_l2a", [128, 16, 8], F32)
    comb_tm, _ = sb("s_combtm", [128, NE], F32)
    s_sb, _ = sb("s_ssb", [128, TFF], F32)
    for val, i in [(1e-5, 64), (RS32, 65), (RS2, 66)]:
        nc.const_aps.register(F32, val, onesf[:, i:i + 1])
        nc.const_aps.register(BF16, val, onesb[:, i:i + 1])
    dbgst, _ = sb("s_dbgst", [128, 48], F32)
    junk, _ = sb("s_junk", [128, 96], F32)
    xnst0, _ = sb("s_xnst0", [128, D], BF16)
    xnst = [xnst0, xnst0]
    P_end = cur[0]

    RA = cur[0]
    q_pad, _ = sb("s_qpad", [128, 4, L], BF16, at=RA)
    k_pad, _ = sb("s_kpad", [128, 4, L], BF16, at=RA + 16384)
    A_T, _ = sb("s_AT", [128, 16, 512], BF16, at=RA + 32768)
    WqTp, _ = sb("s_WqTp", [128, NKD, 16, 32], BF16, at=RA + 49152)
    WkTp, _ = sb("s_WkTp", [128, NKD, 16, 32], BF16, at=RA + 57344)
    acc, _ = sb("s_acc", [128, NKD, L], F32, at=RA)
    wq_n, _ = sb("s_wqn", [128, 2, D], BF16, at=RA + 32768)
    wk_n, _ = sb("s_wkn", [128, 2, D], BF16, at=RA + 32768 + 4096)
    cur[0] = RA + 65536

    RB = cur[0]
    v_aug, _ = sb("s_vaug", [128, TCH, NH, 65], BF16, at=RB)
    y_fm, _ = sb("s_yfm", [128, NKD, L], BF16, at=RB + 33280)
    WvT, _ = sb("s_WvT", [128, NKD, D], BF16, at=RB + 66048)
    wv_n, _ = sb("s_wvn", [128, NKD, D], BF16, at=RB + 33280)
    x_t0, _ = sb("s_xt0", [128, D], F32, at=RB)
    x_t1, _ = sb("s_xt1", [128, D], F32, at=RB + 4096)
    x_t = [x_t0, x_t1]
    WoT, _ = sb("s_WoT", [128, NKD, D], BF16, at=RB)
    wo_n, _ = sb("s_won", [128, NKD, D], BF16, at=RB + 66048)
    a_p, _ = sb("s_ap", [128, NM, TFF], BF16, at=RB)
    w3_e, _ = sb("s_w3e", [128, NM, D], BF16, at=RB + 32768)
    w12s0, _ = sb("s_w12s0", [128, 2, NKD, 128], BF16, at=RB + 65536)
    w12s1, _ = sb("s_w12s1", [128, 2, NKD, 128], BF16, at=RB + 65536 + 4096)
    w12s = [w12s0, w12s1]
    x2_0, _ = sb("s_x20", [128, D], F32, at=RB)
    x2_1, _ = sb("s_x21", [128, D], F32, at=RB + 4096)
    x_t2 = [x2_0, x2_1]
    ost0, _ = sb("s_ost0", [128, D], F32, at=RB + 8192)
    ost1, _ = sb("s_ost1", [128, D], F32, at=RB + 12288)
    outst = [ost0, ost1]
    rep_sb, _ = sb("s_repsb", [128, TFF], F32, at=RB + 73728)
    atmp, _ = sb("s_atmp", [128, TFF], F32, at=RB + 77824)
    cur[0] = RB + 82432
    assert cur[0] <= nc.sbuf_top, (cur[0], nc.sbuf_top)

    pb = [es.enter_context(nc.psum_tensor(f"pb{i}", [128, 512], F32)) for i in range(6)]
    pbt = [es.enter_context(nc.psum_tensor(f"pbt{i}", [128, 1024], BF16)) for i in range(2)]

    sems = {s: es.enter_context(nc.semaphore(f"sem_{s}"))
            for s in ["T", "V", "A", "G", "DG", "DS", "DA"]}

    S = Sched()

    def jg(waits=(), inc=None):
        return S.op("V", lambda e: e.tensor_copy(junk[:, 80:96], onesf[:, 0:16]), waits, inc)

    w_ident_f = S.dma("S", identf[:, :], d["cIdent"][:, :])
    w_ident_b = S.dma("G", identb[:, :], d["cIdent"][:, :])
    w_half = S.dma("G", halfb[:, :, :], d["cHalf"][:, :, :])
    w_sel = S.dma("S", self_[:, :, :], d["cSel"][:, :, :])
    w_onesb = S.dma("G", onesb[:, :], d["cOnes"][:, :])
    w_onesf = S.dma("S", onesf[:, :], d["cOnes"][:, :])
    w_wg = S.dma("G", wg_sb[:, :, :], d["Wg"].rearrange("(k p) e -> p k e", p=128))
    w_wqn = S.dma("G", wq_n[:, :, :], d["Wq"].rearrange("(oc p) dd -> p oc dd", p=128))
    w_wkn = S.dma("G", wk_n[:, :, :], d["Wk"].rearrange("(oc p) dd -> p oc dd", p=128))
    w_wvn = S.dma("G", wv_n[:, :, :], d["Wv"].rearrange("(oc p) dd -> p oc dd", p=128))
    z_wq = S.op("G", lambda e: e.memset(WqTp[:, :, :, :], 0.0))
    z_wk = S.op("G", lambda e: e.memset(WkTp[:, :, :, :], 0.0), inc="G")

    S.op("A", lambda e: e.activation(out=junk[:, 0:4], in_=onesf[:, 0:4],
                                     func=AF.Square, accum_out=junk[:, 32:33]),
         waits=[("DS", w_onesf)])
    S.op("A", lambda e: e.activation(out=junk[:, 4:8], in_=onesf[:, 0:4], func=AF.Sqrt, bias=1e-5))
    S.op("A", lambda e: e.activation(out=junk[:, 8:12], in_=onesf[:, 0:4], func=AF.Identity,
                                     bias=onesf[:, 0:1], scale=onesf[:, 1:2]))
    S.op("A", lambda e: e.activation(out=junk[:, 12:16], in_=onesf[:, 0:4], func=AF.Silu))
    S.op("A", lambda e: e.activation(out=junk[:, 16:20], in_=onesf[:, 0:4], func=AF.Exp), inc="A")

    xn_act = [0] * TCH
    x_load = [0] * TCH
    mean_done = [0] * TCH
    sq_a = [0] * TCH
    tp_done = {}
    ma = stat_t["mean_all"]; s2a = stat_t["s2_all"]; va = stat_t["var_all"]
    sda = stat_t["std_all"]; t16 = stat_t["tmp16"]; s1 = stat_t["s1"][:, 0:1]
    for c in range(TCH):
        wl = []
        if c >= 2:
            wl = [("V", sq_a[c - 2])]
        x_load[c] = S.dma("S", x_t[c % 2][:, :], d["x"][c * 128:(c + 1) * 128, :], waits=wl)
        S.op("V", lambda e, c=c: e.reduce_sum(s1, x_t[c % 2][:, :], axis=AX),
             waits=[("DS", x_load[c])])
        S.op("V", lambda e, c=c: e.tensor_tensor(s_sb[:, 0:D], x_t[c % 2][:, :],
                                                 x_t[c % 2][:, :], ALU.mult))
        S.op("V", lambda e, c=c:
             e.tensor_single_scalar(ma[:, c:c + 1], s1, 1.0 / D, ALU.mult))
        mean_done[c] = sq_a[c] = S.op("V", lambda e, c=c:
                                      e.reduce_sum(s2a[:, c:c + 1], s_sb[:, 0:D], axis=AX),
                                      inc="V")

    S.op("V", lambda e: e.tensor_tensor(t16[:, :], ma[:, :], ma[:, :], ALU.mult))
    S.op("V", lambda e: e.tensor_single_scalar(va[:, :], s2a[:, :], 1.0 / D, ALU.mult))
    jg()
    var_done = S.op("V", lambda e: e.tensor_tensor(va[:, :], va[:, :], t16[:, :], ALU.subtract),
                    inc="V")
    S.op("A", lambda e: e.activation(out=junk[:, 16:80], in_=junk[:, 16:80], func=AF.Square),
         waits=[("V", var_done)])
    std_done = S.op("A", lambda e: e.activation(out=sda[:, :], in_=va[:, :], func=AF.Sqrt,
                                                bias=1e-5), inc="A")
    isda = stat_t["istd_all"]
    S.op("V", lambda e: e.tensor_copy(junk[:, 16:80], s_sb[:, 512:576]),
         waits=[("A", std_done)])
    jg()
    S.op("V", lambda e: e.reciprocal(isda[:, :], sda[:, :]))
    istd_done = S.op("V", lambda e: e.tensor_copy(junk[:, 0:16], onesf[:, 0:16]), inc="V")

    xn_v = [0] * TCH
    for c in range(TCH):
        wl = [("V", istd_done)]
        if c >= 2:
            wl.append(("V", xn_v[c - 2]))
        ld2 = S.dma("S", x_t[c % 2][:, :], d["x"][c * 128:(c + 1) * 128, :], waits=wl)
        mc = stat_t["cell0"][:, 0:1]; ic = stat_t["cell1"][:, 0:1]
        S.op("V", lambda e, c=c, mc=mc: e.tensor_copy(mc, ma[:, c:c + 1]))
        S.op("V", lambda e, c=c, ic=ic: e.tensor_copy(ic, isda[:, c:c + 1]))
        S.op("V", lambda e, c=c, mc=mc: e.tensor_scalar(s_sb[:, 0:D], x_t[c % 2][:, :],
                                                        mc, None, ALU.subtract),
             waits=[("DS", ld2)])
        wx = []
        if c >= 1:
            wx.append(("V", tp_done[("xn", c - 1, NKD - 1)]))
        xn_v[c] = S.op("V", lambda e, c=c, ic=ic: e.tensor_scalar(xnst[c % 2][:, :], s_sb[:, 0:D],
                                                                  ic, None, ALU.mult),
                       waits=wx, inc="V")
        for k in range(NKD):
            bank = pbt[k % 2]
            wt = [("V", xn_v[c]), ("DG", w_ident_b)]
            key = ("xn", c, k - 2)
            if key in tp_done:
                wt.append(("V", tp_done[key]))
            t0 = S.op("T", lambda e, bank=bank, c=c, k=k:
                      e.transpose(bank[:, 0:128], xnst[c % 2][:, k * 128:(k + 1) * 128], identb[:, :]),
                      waits=wt, inc="T")
            tp_done[("xn", c, k)] = S.op(
                "V", lambda e, bank=bank, c=c, k=k:
                e.tensor_copy(xnfm[:, k, c * 128:(c + 1) * 128], bank[:, 0:128]),
                waits=[("T", t0)], inc="V")
        xn_act[c] = S.cnt["A"]
    xn_all = S.cnt["V"]

    def finish_dev(dumps):
        allw = [("V", S.cnt["V"]), ("A", S.cnt["A"]), ("T", S.cnt["T"]), ("G", S.cnt["G"])]
        for nm, ap in dumps:
            S.dma("G", dbg[nm][:, :], ap, waits=allw)
        return True

    if PHASE >= 2:
        def w_transpose(src, n_oc, evac_fn, wait_dma, tag, extra=(), prev=0):
            last = 0
            for oc in range(n_oc):
                for kd in range(NKD):
                    bank = pbt[(oc * NKD + kd) % 2]
                    wt = [("DG", wait_dma)] + list(extra)
                    key = (tag, oc * NKD + kd - 2)
                    if key in tp_done:
                        wt.append(("V", tp_done[key]))
                    elif prev:
                        wt.append(("V", prev))
                    t0 = S.op("T", lambda e, bank=bank, oc=oc, kd=kd, src=src:
                              e.transpose(bank[:, 0:128], src[:, oc, kd * 128:(kd + 1) * 128], identb[:, :]),
                              waits=wt, inc="T")
                    last = tp_done[(tag, oc * NKD + kd)] = S.op(
                        "V", lambda e, bank=bank, oc=oc, kd=kd: evac_fn(e, bank, oc, kd),
                        waits=[("T", t0)], inc="V")
            return last

        wqT_done = w_transpose(
            wq_n, 2,
            lambda e, bank, oc, kd: e.tensor_copy(
                WqTp[:, kd, 8 * oc:8 * oc + 8, 0:16],
                bank[:, 0:128].rearrange("p (h f) -> p h f", h=8)),
            w_wqn, "wq", extra=[("G", z_wk)], prev=xn_all)
        wkT_done = w_transpose(
            wk_n, 2,
            lambda e, bank, oc, kd: e.tensor_copy(
                WkTp[:, kd, 8 * oc:8 * oc + 8, 0:16],
                bank[:, 0:128].rearrange("p (h f) -> p h f", h=8)),
            w_wkn, "wk", extra=[("G", z_wk)], prev=wqT_done)
        wvT_done = w_transpose(
            wv_n, NKD,
            lambda e, bank, oc, kd: e.tensor_copy(
                WvT[:, kd, oc * 128:(oc + 1) * 128], bank[:, 0:128]),
            w_wvn, "wv", prev=wkT_done)

        proj_done = {}

        def project_qk(WT, dst, wT_done, zero_done, tag, bank_guard=0):
            last = 0
            for hc in range(4):
                for tq in range(NTQ):
                    bank = pb[4 + (hc * NTQ + tq) % 2]
                    wt = [("V", wT_done), ("V", xn_all), ("G", zero_done)]
                    if bank_guard:
                        wt.append(("V", bank_guard))
                    key = (tag, hc * NTQ + tq - 2)
                    if key in proj_done:
                        wt.append(("V", proj_done[key]))
                    mm = 0
                    for k in range(NKD):
                        mm = S.op("T", lambda e, bank=bank, k=k, hc=hc, tq=tq, WT=WT:
                                  e.matmul(bank[:, :], WT.ap().rearrange("p a b c -> p a (b c)")[:, k, hc * 128:(hc + 1) * 128],
                                           xnfm[:, k, tq * 512:(tq + 1) * 512],
                                           start=(k == 0), stop=(k == NKD - 1)),
                                  waits=(wt if k == 0 else []), inc=("T" if k == NKD - 1 else None))
                    last = proj_done[(tag, hc * NTQ + tq)] = S.op(
                        "V", lambda e, bank=bank, hc=hc, tq=tq, dst=dst:
                        e.tensor_copy(dst[:, hc, tq * 512:(tq + 1) * 512], bank[:, :]),
                        waits=[("T", mm)], inc="V")
            return last

        qp_done = project_qk(WqTp, q_pad, wqT_done, z_wk, "q")
        kp_done = project_qk(WkTp, k_pad, wkT_done, z_wk, "k", bank_guard=qp_done)

        vones = S.op("G", lambda e: e.memset(v_aug[:, :, :, 64:65], 1.0),
                     waits=[("V", xn_v[TCH - 1])], inc="G")

        for c in range(TCH):
            for vh in range(2):
                bank = pb[2 + (c * 2 + vh) % 2]
                wt = [("V", wvT_done), ("V", xn_all), ("G", vones)]
                key = ("v", c * 2 + vh - 2)
                if key in proj_done:
                    wt.append(("V", proj_done[key]))
                mm = 0
                for k in range(NKD):
                    mm = S.op("T", lambda e, bank=bank, k=k, c=c, vh=vh:
                              e.matmul(bank[:, :], xnfm[:, k, c * 128:(c + 1) * 128],
                                       WvT[:, k, vh * 512:(vh + 1) * 512],
                                       start=(k == 0), stop=(k == NKD - 1)),
                              waits=(wt if k == 0 else []), inc=("T" if k == NKD - 1 else None))
                proj_done[("v", c * 2 + vh)] = S.op(
                    "V", lambda e, bank=bank, c=c, vh=vh:
                    e.tensor_copy(v_aug[:, c, 8 * vh:8 * vh + 8, 0:64],
                                  bank[:, :].rearrange("p (h x) -> p h x", h=8)),
                    waits=[("T", mm)], inc="V")
        v_done = S.cnt["V"]

    if PHASE >= 3:
        for c in range(TCH):
            bank = pb[(c % 2)]
            wt = [("DG", w_wg), ("V", xn_all)]
            key = ("lg", c - 2)
            if key in proj_done:
                wt.append(("V", proj_done[key]))
            mm = 0
            for k in range(NKD):
                mm = S.op("T", lambda e, bank=bank, k=k, c=c:
                          e.matmul(bank[:, 0:NE], xnfm[:, k, c * 128:(c + 1) * 128],
                                   wg_sb[:, k, :], start=(k == 0), stop=(k == NKD - 1)),
                          waits=(wt if k == 0 else []), inc=("T" if k == NKD - 1 else None))
            proj_done[("lg", c)] = S.op(
                "V", lambda e, bank=bank, c=c: e.tensor_copy(lg_tm[:, c, :], bank[:, 0:NE]),
                waits=[("T", mm)], inc="V")

        gm1 = stat_t["gm1"]; gm2 = stat_t["gm2"]; ge2 = stat_t["ge2"]
        gden = stat_t["gden"]; gw1 = stat_t["gw1"]; gw2 = stat_t["gw2"]
        comb_done = [0] * TCH
        ce = [stat_t["cell0"][:, 0:1], stat_t["cell1"][:, 0:1]]
        def gate_tail(c):
            S.op("V", lambda e, c=c: e.tensor_copy(ce[c % 2], gm1[:, c:c + 1]))
            jg()
            S.op("V", lambda e, c=c: e.tensor_scalar(eq1_all[:, c, :], lg_tm[:, c, :],
                                                     ce[c % 2], None, ALU.is_equal))
            jg()
            S.op("V", lambda e, c=c: e.scalar_tensor_tensor(l2_all[:, c, :], eq1_all[:, c, :],
                                                            -1e30, lg_tm[:, c, :],
                                                            ALU.mult, ALU.add))
            jg()
            S.op("V", lambda e, c=c: e.reduce_max(gm2[:, c:c + 1], l2_all[:, c, :], axis=AX))
        for c in range(TCH):
            S.op("V", lambda e, c=c: e.reduce_max(gm1[:, c:c + 1], lg_tm[:, c, :], axis=AX),
                 waits=[("V", proj_done[("lg", c)])])
            if c >= 1:
                gate_tail(c - 1)
        gate_tail(TCH - 1)
        jg()
        gv0 = S.op("V", lambda e: e.tensor_tensor(ge2[:, :], gm2[:, :], gm1[:, :], ALU.subtract),
                   inc="V")
        S.op("A", lambda e: e.activation(out=junk[:, 16:80], in_=junk[:, 16:80], func=AF.Square),
             waits=[("V", gv0)])
        ga2 = S.op("A", lambda e: e.activation(out=ge2[:, :], in_=ge2[:, :], func=AF.Exp), inc="A")
        S.op("V", lambda e: e.tensor_copy(junk[:, 16:80], s_sb[:, 512:576]), waits=[("A", ga2)])
        S.op("V", lambda e: e.tensor_single_scalar(gden[:, :], ge2[:, :], 1.0, ALU.add))
        jg()
        S.op("V", lambda e: e.reciprocal(gw1[:, :], gden[:, :]))
        S.op("V", lambda e: e.tensor_copy(junk[:, 0:16], ge2[:, :]))
        gv1 = S.op("V", lambda e: e.tensor_tensor(gw2[:, :], ge2[:, :], gw1[:, :], ALU.mult),
                   inc="V")
        ce0 = stat_t["cell0"][:, 0:1]
        ce1 = stat_t["cell1"][:, 0:1]
        ce2 = stat_t["s1"][:, 0:1]
        for c in range(TCH):
            S.op("V", lambda e, c=c: e.tensor_copy(ce0, gm2[:, c:c + 1]))
            S.op("V", lambda e, c=c: e.tensor_copy(ce1, gw1[:, c:c + 1]))
            S.op("V", lambda e, c=c: e.tensor_scalar(l2_all[:, c, :], l2_all[:, c, :],
                                                     ce0, None, ALU.is_equal))
            S.op("V", lambda e, c=c: e.tensor_copy(ce2, gw2[:, c:c + 1]))
            S.op("V", lambda e, c=c: e.tensor_scalar(comb_tm[:, :], eq1_all[:, c, :],
                                                     ce1, None, ALU.mult))
            S.op("V", lambda e, c=c: e.tensor_scalar(l2_all[:, c, :], l2_all[:, c, :],
                                                     ce2, None, ALU.mult))
            jg()
            S.op("V", lambda e, c=c: e.tensor_tensor(comb_tm[:, :], comb_tm[:, :],
                                                     l2_all[:, c, :], ALU.add))
            v6 = jg(inc="V")
            bank = pb[2 + c % 2]
            wt = [("V", v6), ("DS", w_ident_f), ("V", v_done)]
            if c >= 2:
                wt.append(("V", comb_done[c - 2]))
            t1 = S.op("T", lambda e, bank=bank: e.transpose(bank[0:NE, 0:128], comb_tm[:, :], identf[:, :]),
                      waits=wt, inc="T")
            comb_done[c] = S.op("V", lambda e, bank=bank, c=c:
                                e.tensor_copy(comb_em[:, c, :], bank[0:32, 0:128]),
                                waits=[("T", t1)], inc="V")
        gate_done = S.cnt["V"]

    if PHASE >= 4:
        sq_done = {}
        at_read = [0] * 16
        sbank_read = [0] * 4
        ybank_read = 0
        pb5_read = 0
        yfm_done = 0
        for h in range(NH):
            hc, sp = h // 4, 32 * (h % 4)
            even = (h % 2 == 0)
            for tq in range(NTQ):
                ncks = 4 * (tq + 1)
                def emit_s(ck):
                    bank = pb[ck % 4]
                    wt = [("V", qp_done), ("V", kp_done)]
                    if sbank_read[ck % 4]:
                        wt.append(("A", sbank_read[ck % 4]))
                    else:
                        wt.append(("V", gate_done))
                    if at_read[ck]:
                        wt.append(("T", at_read[ck]))
                    t0 = S.op("T", lambda e, bank=bank, ck=ck, hc=hc, sp=sp, tq=tq:
                              e.matmul(bank[:, :], k_pad[sp:sp + 32, hc, ck * 128:(ck + 1) * 128],
                                       q_pad[sp:sp + 32, hc, tq * 512:(tq + 1) * 512],
                                       start=True, stop=True, tile_position=(sp, 0)),
                              waits=wt, inc="T")
                    aev = S.op("A", lambda e, bank=bank, ck=ck:
                               e.activation(out=A_T[:, ck, :], in_=bank[:, :], func=AF.Square,
                                            scale=RS32, bias=RS2),
                               waits=[("T", t0), ("DS", w_onesf)], inc="A")
                    sbank_read[ck % 4] = aev
                    sq_done[ck] = aev
                    if ck >= 4 * tq:
                        sq_done[ck] = None
                        mv = S.op("V", lambda e, ck=ck, tq=tq:
                                  e.scalar_tensor_tensor(A_T[:, ck, :], halfb[:, ck - 4 * tq, :],
                                                         2.0, A_T[:, ck, :], ALU.mult, ALU.mult),
                                  waits=[("A", aev), ("DG", w_half)], inc="V")
                        sq_done[ck] = ("V", mv)
                    else:
                        sq_done[ck] = ("A", aev)

                for ck in range(min(4, ncks)):
                    emit_s(ck)
                yb = pb[4]
                zb = pb[5]
                last_mm = 0
                for ck in range(ncks):
                    if ck + 4 < ncks:
                        emit_s(ck + 4)
                    cslot = (ck - 4 * tq) if ck >= 4 * tq else 4
                    wt = [sq_done[ck], ("V", v_done), ("DG", w_half)]
                    if ck == 0:
                        if ybank_read:
                            wt.append(("V", ybank_read))
                        if not even and pb5_read:
                            wt.append(("V", pb5_read))
                    st, sp_ = (ck == 0), (ck == ncks - 1)
                    if even:
                        S.op("T", lambda e, yb=yb, ck=ck, h=h, st=st:
                             e.matmul(yb[0:65, :], v_aug[:, ck, h, 0:65], A_T[:, ck, :],
                                      start=st, stop=False),
                             waits=wt, inc=None)
                        last_mm = S.op("T", lambda e, yb=yb, ck=ck, h=h, cslot=cslot, sp_=sp_:
                                       e.matmul(yb[0:65, :], v_aug[:, ck, h, 0:65],
                                                halfb[:, cslot, :], start=False, stop=sp_),
                                       inc=("T" if sp_ else None))
                    else:
                        S.op("T", lambda e, yb=yb, ck=ck, h=h, st=st:
                             e.matmul(yb[64:128, :], v_aug[:, ck, h, 0:64], A_T[:, ck, :],
                                      start=st, stop=False, tile_position=(0, 64)),
                             waits=wt, inc=None)
                        S.op("T", lambda e, yb=yb, ck=ck, h=h, cslot=cslot:
                             e.matmul(yb[64:128, :], v_aug[:, ck, h, 0:64],
                                      halfb[:, cslot, :], start=False, stop=False, tile_position=(0, 64)),
                             inc=None)
                        S.op("T", lambda e, zb=zb, ck=ck, st=st:
                             e.matmul(zb[0:1, :], onesb[:, 0:1], A_T[:, ck, :],
                                      start=st, stop=False),
                             waits=([("DG", w_onesb)] if (h == 1 and tq == 0 and ck == 0) else []), inc=None)
                        last_mm = S.op("T", lambda e, zb=zb, ck=ck, cslot=cslot, sp_=sp_:
                                       e.matmul(zb[0:1, :], onesb[:, 0:1],
                                                halfb[:, cslot, :], start=False, stop=sp_),
                                       inc=("T" if sp_ else None))
                for ck in range(ncks):
                    at_read[ck] = last_mm
                if even:
                    zrow_p, zrow_s = yb[64:65, :], zbuf[64:65, :]
                    ones_l = onesf[64:65, 0:64]
                    tp = (64, 0)
                    rep_rows = slice(0, 64)
                    y_rows = slice(0, 64)
                else:
                    zrow_p, zrow_s = zb[0:1, :], zbuf[0:1, :]
                    ones_l = onesf[0:1, 0:64]
                    tp = (0, 64)
                    rep_rows = slice(64, 128)
                    y_rows = slice(64, 128)
                v7 = S.op("V", lambda e, zrow_p=zrow_p, zrow_s=zrow_s:
                          e.tensor_single_scalar(zrow_s, zrow_p, 1e-12, ALU.add),
                          waits=[("T", last_mm)])
                jg()
                S.op("V", lambda e, zrow_s=zrow_s: e.reciprocal(zrow_s, zrow_s))
                v8 = jg(inc="V")
                rb_ = pb[5]
                wt = [("V", v8), ("DS", w_onesf)]
                if even and pb5_read:
                    wt.append(("V", pb5_read))
                t2 = S.op("T", lambda e, rb_=rb_, ones_l=ones_l, zrow_s=zrow_s, tp=tp, rep_rows=rep_rows:
                          e.matmul(rb_[rep_rows, :], ones_l, zrow_s, start=True, stop=True,
                                   tile_position=tp),
                          waits=wt, inc="T")
                v9 = S.op("V", lambda e, rb_=rb_, rep_rows=rep_rows:
                          e.tensor_copy(zrep[rep_rows, :], rb_[rep_rows, :]),
                          waits=[("T", t2)], inc="V")
                pb5_read = v9
                yfm_done = S.op("V", lambda e, yb=yb, h=h, tq=tq, y_rows=y_rows:
                                e.tensor_tensor(y_fm[y_rows, h // 2, tq * 512:(tq + 1) * 512],
                                                yb[y_rows, :], zrep[y_rows, :], ALU.mult),
                                waits=[("V", v9)], inc="V")
                ybank_read = yfm_done

        w_won = S.dma("G", wo_n[:, :, :], d["Wo"].rearrange("(oc p) dd -> p oc dd", p=128),
                      waits=[("V", v_done)])
        last_T_attn = S.cnt["T"]
        woT_done = 0
        for oc in range(NKD):
            for kd in range(NKD):
                i = oc * NKD + kd
                bank = pbt[i % 2]
                wt = [("DG", w_won), ("T", last_T_attn)]
                key = ("wo", i - 2)
                if key in tp_done:
                    wt.append(("V", tp_done[key]))
                t0 = S.op("T", lambda e, bank=bank, oc=oc, kd=kd:
                          e.transpose(bank[:, 0:128], wo_n[:, oc, kd * 128:(kd + 1) * 128], identb[:, :]),
                          waits=wt, inc="T")
                woT_done = tp_done[("wo", i)] = S.op(
                    "V", lambda e, bank=bank, oc=oc, kd=kd:
                    e.tensor_copy(WoT[:, kd, oc * 128:(oc + 1) * 128], bank[:, 0:128]),
                    waits=[("T", t0)], inc="V")

        oproj_done = {}
        for dchunk in range(NKD):
            for ts in range(NTQ):
                i = dchunk * NTQ + ts
                bank = pb[2 + i % 2]
                wt = [("V", woT_done), ("V", yfm_done)]
                key = ("op", i - 2)
                if key in oproj_done:
                    wt.append(("V", oproj_done[key]))
                mm = 0
                for kvd in range(NKD):
                    mm = S.op("T", lambda e, bank=bank, kvd=kvd, dchunk=dchunk, ts=ts:
                              e.matmul(bank[:, :], WoT[:, kvd, dchunk * 128:(dchunk + 1) * 128],
                                       y_fm[:, kvd, ts * 512:(ts + 1) * 512],
                                       start=(kvd == 0), stop=(kvd == NKD - 1)),
                              waits=(wt if kvd == 0 else []), inc=("T" if kvd == NKD - 1 else None))
                oproj_done[("op", i)] = S.op(
                    "V", lambda e, bank=bank, dchunk=dchunk, ts=ts:
                    e.tensor_copy(acc[:, dchunk, ts * 512:(ts + 1) * 512], bank[:, :]),
                    waits=[("T", mm)], inc="V")
        oproj_all = S.cnt["V"]

    if PHASE >= 5:
        w12_read = [0, 0]
        w3_read = 0
        ybank2 = {4: 0, 5: 0}
        silu_done = [0, 0]
        ga_done = [0, 0]
        rep_done = 0
        for tf in range(NTF):
            for e_i in range(NE):
                rpd = 0
                for sub in range(NSUB):
                    rbank = 4 + sub
                    cw = [("V", gate_done), ("DS", w_sel), ("V", oproj_all)]
                    if ybank2[rbank]:
                        cw.append(("V", ybank2[rbank]))
                    t3 = S.op("T", lambda e, rbank=rbank, e_i=e_i, tf=tf, sub=sub:
                              e.matmul(pb[rbank][:, :], self_[:, e_i, :],
                                       comb_em.ap().rearrange("p a b -> p (a b)")[0:8, (tf * 2 + sub) * 512:(tf * 2 + sub + 1) * 512],
                                       start=True, stop=True),
                              waits=cw, inc="T")
                    rpd = S.op("V", lambda e, rbank=rbank, sub=sub:
                               e.tensor_copy(rep_sb[:, sub * 512:(sub + 1) * 512], pb[rbank][:, :]),
                               waits=[("T", t3)], inc="V")
                    ybank2[rbank] = rpd
                rep_done = rpd
                dw3 = 0
                for m in range(NM):
                    wvw = [("V", oproj_all)]
                    if w12_read[m % 2]:
                        wvw.append(("T", w12_read[m % 2]))
                    dw12 = S.dma("G", w12s[m % 2][:, :, :, :],
                                 d["W12r"][e_i, m].rearrange("w p k j -> p w k j"), waits=wvw)
                    w3w = [("V", oproj_all)]
                    if w3_read:
                        w3w.append(("T", w3_read))
                    dw3 = S.dma("G", w3_e[:, m, :], d["W3r"][e_i, m], waits=w3w)
                    last_g = 0
                    for sub in range(NSUB):
                        u = (m * NSUB + sub) % 2
                        hbk, gbk = pb[0 + 2 * u], pb[1 + 2 * u]
                        wt = [("DG", dw12)]
                        if silu_done[u]:
                            wt.append(("A", silu_done[u]))
                        hmm = 0
                        for k in range(NKD):
                            hmm = S.op("T", lambda e, hbk=hbk, sub=sub, k=k, tf=tf, m=m:
                                       e.matmul(hbk[:, :], w12s[m % 2][:, 0, k, :],
                                                xnfm[:, k, tf * TFF + sub * 512:tf * TFF + (sub + 1) * 512],
                                                start=(k == 0), stop=(k == NKD - 1)),
                                       waits=(wt if k == 0 else []),
                                       inc=("T" if k == NKD - 1 else None))
                        wtg = []
                        if ga_done[u]:
                            wtg.append(("V", ga_done[u]))
                        gmm = 0
                        for k in range(NKD):
                            gmm = S.op("T", lambda e, gbk=gbk, sub=sub, k=k, tf=tf, m=m:
                                       e.matmul(gbk[:, :], w12s[m % 2][:, 1, k, :],
                                                xnfm[:, k, tf * TFF + sub * 512:tf * TFF + (sub + 1) * 512],
                                                start=(k == 0), stop=(k == NKD - 1)),
                                       waits=(wtg if k == 0 else []),
                                       inc=("T" if k == NKD - 1 else None))
                        last_g = gmm
                        sld = S.op("A", lambda e, hbk=hbk, sub=sub:
                                   e.activation(out=s_sb[:, sub * 512:(sub + 1) * 512],
                                                in_=hbk[:, :], func=AF.Silu),
                                   waits=[("T", hmm)], inc="A")
                        S.op("V", lambda e, gbk=gbk, sub=sub:
                             e.tensor_tensor(atmp[:, sub * 512:(sub + 1) * 512],
                                             s_sb[:, sub * 512:(sub + 1) * 512], gbk[:, :], ALU.mult),
                             waits=[("A", sld), ("T", gmm), ("V", rep_done)])
                        gad = S.op("V", lambda e, m=m, sub=sub:
                                   e.tensor_tensor(a_p[:, m, sub * 512:(sub + 1) * 512],
                                                   atmp[:, sub * 512:(sub + 1) * 512],
                                                   rep_sb[:, sub * 512:(sub + 1) * 512], ALU.mult),
                                   inc="V")
                        silu_done[u] = sld
                        ga_done[u] = gad
                    w12_read[m % 2] = last_g
                a_all = S.cnt["V"]
                ymm = 0
                for dchunk in range(NKD):
                    for sub in range(NSUB):
                        yb2i = 4 + ((dchunk * NSUB + sub) % 2)
                        wt = [("V", a_all), ("DG", dw3)]
                        if ybank2[yb2i]:
                            wt.append(("V", ybank2[yb2i]))
                        mm = 0
                        for m in range(NM):
                            mm = S.op("T", lambda e, yb2i=yb2i, m=m, dchunk=dchunk, sub=sub:
                                      e.matmul(pb[yb2i][:, :], w3_e[:, m, dchunk * 128:(dchunk + 1) * 128],
                                               a_p[:, m, sub * 512:(sub + 1) * 512],
                                               start=(m == 0), stop=(m == NM - 1)),
                                      waits=(wt if m == 0 else []), inc=("T" if m == NM - 1 else None))
                        ymm = mm
                        ybank2[yb2i] = S.op(
                            "V", lambda e, yb2i=yb2i, dchunk=dchunk, sub=sub, tf=tf:
                            e.tensor_tensor(acc[:, dchunk, tf * TFF + sub * 512:tf * TFF + (sub + 1) * 512],
                                            acc[:, dchunk, tf * TFF + sub * 512:tf * TFF + (sub + 1) * 512],
                                            pb[yb2i][:, :], ALU.add),
                            waits=[("T", mm)], inc="V")
                w3_read = ymm
        ffn_all = S.cnt["V"]

    if PHASE >= 99:
        out_dma = [0] * TCH
        fin_tp = {}
        for c in range(TCH):
            wl = [("V", ffn_all)]
            if c >= 2:
                wl.append(("DS", out_dma[c - 2]))
            xl = S.dma("S", x_t2[c % 2][:, :], d["x"][c * 128:(c + 1) * 128, :], waits=wl)
            vv = 0
            for k in range(NKD):
                i = c * NKD + k
                bank = pb[i % 2]
                wt = [("V", ffn_all), ("DS", w_ident_f)]
                key = ("fin", i - 2)
                if key in fin_tp:
                    wt.append(("V", fin_tp[key]))
                t0 = S.op("T", lambda e, bank=bank, c=c, k=k:
                          e.transpose(bank[:, 0:128], acc[:, k, c * 128:(c + 1) * 128], identf[:, :]),
                          waits=wt, inc="T")
                vv = fin_tp[("fin", i)] = S.op(
                    "V", lambda e, bank=bank, c=c, k=k:
                    e.tensor_tensor(outst[c % 2][:, k * 128:(k + 1) * 128],
                                    x_t2[c % 2][:, k * 128:(k + 1) * 128], bank[:, 0:128], ALU.add),
                    waits=[("T", t0), ("DS", xl)], inc="V")
            out_dma[c] = S.dma("S", d["out"][c * 128:(c + 1) * 128, :], outst[c % 2][:, :],
                               waits=[("V", vv)])

    
    if PHASE < 99:
        S.op("V", lambda e: e.tensor_copy(dbgst[:, 0:16], stat_t["mean_all"][:, :]))
        S.op("V", lambda e: e.tensor_copy(dbgst[:, 16:32], stat_t["istd_all"][:, :]))
        S.op("V", lambda e: e.tensor_copy(dbgst[:, 32:48], stat_t["var_all"][:, :]), inc="V")
        dumps = []
        if PHASE >= 1:
            dumps.append(("dbg_xnfm", xnfm.ap().rearrange("p a b -> p (a b)")))
            dumps.append(("dbg_st", dbgst[:, :]))
        if PHASE >= 2:
            dumps.append(("dbg_q", q_pad.ap().rearrange("p a b -> p (a b)")))
            dumps.append(("dbg_k", k_pad.ap().rearrange("p a b -> p (a b)")))
            dumps.append(("dbg_v", v_aug.ap().rearrange("p a b c -> p (a b c)")))
        if PHASE >= 3:
            dumps.append(("dbg_comb", comb_em.ap().rearrange("p a b -> p (a b)")))
        if PHASE >= 4:
            dumps.append(("dbg_yfm", y_fm.ap().rearrange("p a b -> p (a b)")))
            dumps.append(("dbg_acc", acc.ap().rearrange("p a b -> p (a b)")))
        if PHASE >= 5 and ("dbg_acc", ) not in dumps:
            pass
        finish_dev(dumps)

    eng_names = {"T": "tensor", "V": "vector", "A": "scalar", "G": "gpsimd", "S": "sync"}
    with nc.Block() as block:
        def make_section(short):
            def body(eng):
                last_wait = {}
                own = ENG_SEM.get(short)
                for (en, fn, ws, inc) in S.recs:
                    if en != short:
                        continue
                    for (s, v) in ws:
                        if s == own:
                            continue
                        if last_wait.get(s, 0) < v:
                            eng.wait_ge(sems[s], v)
                            last_wait[s] = v
                    instr = fn(eng)
                    if inc is not None:
                        instr.then_inc(sems[inc], 16 if inc.startswith("D") else 1)
                if short == "S":
                    if S.cnt["DS"]:
                        eng.wait_ge(sems["DS"], S.cnt["DS"])
                    if S.cnt["DG"]:
                        eng.wait_ge(sems["DG"], S.cnt["DG"])
            return body
        for short, name in eng_names.items():
            getattr(block, name)(make_section(short))

    es.close()
    return nc


def _host_layouts(W1, W2, W3):
    W1r = np.ascontiguousarray(
        W1.reshape(NE, NKD, 128, NM, 128).transpose(0, 3, 2, 1, 4))
    W2r = np.ascontiguousarray(
        W2.reshape(NE, NKD, 128, NM, 128).transpose(0, 3, 2, 1, 4))
    W12r = np.ascontiguousarray(np.stack([W1r, W2r], axis=2))
    W3r = np.ascontiguousarray(W3.reshape(NE, NM, 128, D))
    return W12r, W3r


def _consts():
    ident = np.eye(128, dtype=np.float32)
    p = np.arange(128)[:, None]
    j = np.arange(512)[None, :]
    half = np.zeros((128, 5, 512), dtype=np.float32)
    for c in range(4):
        half[:, c, :] = 0.5 * (j >= c * 128 + p).astype(np.float32)
    half[:, 4, :] = 0.5
    sel = np.zeros((8, NE, 128), dtype=np.float32)
    for e in range(NE):
        sel[e, e, :] = 1.0
    ones = np.ones((128, 68), dtype=np.float32)
    ones[:, 64] = 1e-5
    ones[:, 65] = RS32
    ones[:, 66] = RS2
    ones[:, 67] = 0.0
    return ident, half, sel, ones


def kernel(L0_attn_hidden_states, L0_attn_Wq, L0_attn_Wk, L0_attn_Wv, L0_attn_Wo,
           L0_ffn_Wg, L0_ffn_W1, L0_ffn_W2, L0_ffn_W3):
    x = np.ascontiguousarray(np.asarray(L0_attn_hidden_states, dtype=np.float32))
    Wq = np.ascontiguousarray(np.asarray(L0_attn_Wq, dtype=np.float32))
    Wk = np.ascontiguousarray(np.asarray(L0_attn_Wk, dtype=np.float32))
    Wv = np.ascontiguousarray(np.asarray(L0_attn_Wv, dtype=np.float32))
    Wo = np.ascontiguousarray(np.asarray(L0_attn_Wo, dtype=np.float32))
    Wg = np.ascontiguousarray(np.asarray(L0_ffn_Wg, dtype=np.float32))
    W12r, W3r = _host_layouts(np.asarray(L0_ffn_W1, dtype=np.float32),
                              np.asarray(L0_ffn_W2, dtype=np.float32),
                              np.asarray(L0_ffn_W3, dtype=np.float32))
    ident, half, sel, ones = _consts()

    nc = build()
    shared = {"Wq": Wq, "Wk": Wk, "Wv": Wv, "Wo": Wo, "Wg": Wg,
              "W12r": W12r, "W3r": W3r, "cIdent": ident,
              "cHalf": half, "cSel": sel, "cOnes": ones}
    in_maps = [dict(shared, x=np.ascontiguousarray(x[i])) for i in range(8)]
    res = run_bass_kernel_spmd(nc, in_maps, core_ids=list(range(8)))
    return np.stack([res.results[i]["out"] for i in range(8)], axis=0)
